# version 05
# baseline (speedup 1.0000x reference)
import numpy as np
import ml_dtypes
from contextlib import ExitStack

import concourse.bass as bass
import concourse.mybir as mybir
from concourse.bass_utils import run_bass_kernel_spmd

F32 = mybir.dt.float32
BF16 = mybir.dt.bfloat16
F16 = mybir.dt.float16
I8 = mybir.dt.int8
AF = mybir.ActivationFunctionType
ALU = mybir.AluOpType
AX = mybir.AxisListType

NCORES = 8
P = 128
TOK = 1024
DM = 2048
NDC = DM // P
NH = 16
DK = 128
CH = 64
NCH = TOK // CH
HF = 5632
NHC = HF // P
NE = 8
R = 16
SEQ = 8192
ATT_SCALE = float(DK) ** -0.5
MAGIC = 12582912.0
LNEPS = 1e-5
RMSEPS = 1e-5
QEPS = 1e-7
BIGNEG = 1e30

N_PHASES = 4
DBG_SKIP_LN = False
DBG_SKIP_LN2 = False
DBG_SKIP_LN3 = False


class Sched:
    ENGS = ("sync", "gpsimd", "vector", "scalar", "tensor")

    def __init__(self, nc, es, block):
        self.nc = nc
        self.block = block
        self.pend = {e: [] for e in self.ENGS}
        self.cnt = {}
        self.sems = {}
        for name in ("LD", "WT", "CC", "TE", "VE", "SE", "ST"):
            self.sems[name] = es.enter_context(nc.semaphore(name))
            self.cnt[name] = 0

    def now(self, sem):
        return self.cnt[sem]

    def allnow(self):
        return [(s, self.cnt[s]) for s in self.sems]

    vhazard = 0
    vserial = False

    def emit(self, eng, fn, wait=(), inc=(), vspecial=False):
        wait = list(wait)
        if eng == "vector" and (self.vhazard or self.vserial):
            wait.append(("VE", max(self.vhazard,
                                   self.cnt["VE"] if self.vserial else 0)))
            self.vhazard = 0
        waitl = [(self.sems[s], v) for (s, v) in wait if v > 0]
        for (s, k) in inc:
            self.cnt[s] += k
        self.pend[eng].append((fn, waitl, [(self.sems[s], k) for (s, k) in inc]))
        if vspecial:
            assert eng == "vector" and any(s == "VE" for (s, k) in inc)
            self.vhazard = self.cnt["VE"]

    def dma(self, eng, out, in_, wait=(), sem=None):
        sem = sem or ("LD" if eng in ("sync", "scalar") else "WT")
        wait = list(wait)
        if self.cnt["CC"] > 0:
            wait.append(("CC", self.cnt["CC"]))
        self.emit(eng, lambda e, o=out, i=in_: e.dma_start(out=o, in_=i),
                  wait=wait, inc=[(sem, 16)])
        return self.cnt[sem]

    def collective(self, fn, wait=()):
        w = list(wait) + [("WT", self.cnt["WT"]), ("LD", self.cnt["LD"]),
                          ("CC", self.cnt["CC"])]
        self.emit("gpsimd", fn, wait=w, inc=[("CC", 1)])
        return self.cnt["CC"]

    def barrier(self):
        w = self.allnow()
        for e in self.ENGS:
            self.emit(e, lambda eh: None, wait=w)

    def flush(self):
        for name in self.ENGS:
            ops = self.pend[name]
            if not ops:
                continue
            self.pend[name] = []

            def body(eh, ops=ops):
                for (fn, waits, incs) in ops:
                    for (sh, v) in waits:
                        eh.wait_ge(sh, v)
                    inst = fn(eh)
                    if incs:
                        if inst is None:
                            inst = eh.nop()
                        for (sh, k) in incs:
                            inst.then_inc(sh, k)
            getattr(self.block, name)(body)


def _host_prep(inputs):
    f = np.ascontiguousarray
    I = inputs
    h = {}
    wqT = np.asarray(I["L0_attn_wq"]).T
    wkT = np.asarray(I["L0_attn_wk"]).T
    wvT = np.asarray(I["L0_attn_wv"]).T
    h["w0q"] = f(wqT.reshape(DM, NH, DK).transpose(1, 0, 2).reshape(NH * DM, DK))
    h["w0k"] = f(wkT.reshape(DM, NH, DK).transpose(1, 0, 2).reshape(NH * DM, DK))
    h["w0v"] = f(wvT.reshape(DM, NH, DK).transpose(1, 0, 2).reshape(NH * DM, DK))
    h["fq1T"] = f(np.asarray(I["L0_attn_fq1_w"]).T)
    h["fq2T"] = f(np.asarray(I["L0_attn_fq2_w"]).T)
    h["fk1T"] = f(np.asarray(I["L0_attn_fk1_w"]).T)
    h["fk2T"] = f(np.asarray(I["L0_attn_fk2_w"]).T)
    h["fq1b"] = f(np.asarray(I["L0_attn_fq1_b"]).reshape(DK, 1))
    h["fq2b"] = f(np.asarray(I["L0_attn_fq2_b"]).reshape(DK, 1))
    h["fk1b"] = f(np.asarray(I["L0_attn_fk1_b"]).reshape(DK, 1))
    h["fk2b"] = f(np.asarray(I["L0_attn_fk2_b"]).reshape(DK, 1))
    h["normw"] = f(np.asarray(I["L0_attn_norm_w"]).reshape(DK, 1))
    woT = np.asarray(I["L0_attn_wo"]).T
    h["w0o"] = f(woT.reshape(DM, NDC, P).transpose(1, 0, 2).reshape(NDC * DM, P))
    h["gwT"] = f(np.asarray(I["L0_ffn_gate_w"]).T)
    h["wup"] = f(np.asarray(I["L0_ffn_w_up"]).T.reshape(DM, NHC, P)
                 .transpose(1, 0, 2).reshape(NHC * DM, P))
    h["wg"] = f(np.asarray(I["L0_ffn_w_gate"]).T.reshape(DM, NHC, P)
                .transpose(1, 0, 2).reshape(NHC * DM, P))
    h["wd"] = f(np.asarray(I["L0_ffn_w_down"]).T.reshape(HF, NDC, P)
                .transpose(1, 0, 2).reshape(NDC * HF, P))
    h["upA"] = f(np.asarray(I["L0_ffn_up_A"]).reshape(NE, R, NDC, P)
                 .transpose(3, 0, 2, 1).reshape(P, NE * NDC * R))
    h["gA"] = f(np.asarray(I["L0_ffn_gate_A"]).reshape(NE, R, NDC, P)
                .transpose(3, 0, 2, 1).reshape(P, NE * NDC * R))
    upB = np.asarray(I["L0_ffn_up_B"]).reshape(NE, NHC, P, R).transpose(1, 3, 0, 2)
    h["upB"] = f(upB.reshape(NHC * R, NE * P))
    gB = np.asarray(I["L0_ffn_gate_B"]).reshape(NE, NHC, P, R).transpose(1, 3, 0, 2)
    h["gB"] = f(gB.reshape(NHC * R, NE * P))
    dap = np.zeros((NHC, P, 8 * 32), dtype=np.float32)
    dA = np.asarray(I["L0_ffn_down_A"])
    for e in range(NE):
        dap[:, :, 32 * e:32 * e + R] = dA[e].T.reshape(NHC, P, R)
    h["dAp"] = f(dap.reshape(NHC * P, 256))
    dB = np.asarray(I["L0_ffn_down_B"])
    dba = np.zeros((2, P, DM), dtype=np.float32)
    for e in range(NE):
        dba[e // 4, 32 * (e % 4):32 * (e % 4) + R, :] = dB[e].T
    h["dBTA"] = f(dba[0])
    h["dBTB"] = f(dba[1])
    h["w1qkv"] = f(np.asarray(I["L1_attn_w_qkv"]).reshape(DM, 48, P)
                   .transpose(1, 0, 2).reshape(48 * DM, P))
    h["w1o"] = f(np.asarray(I["L1_attn_w_o"]).reshape(DM, NDC, P)
                 .transpose(1, 0, 2).reshape(NDC * DM, P))
    h["w1T"] = f(np.asarray(I["L1_ffn_weight"]).T.reshape(DM, NDC, P)
                 .transpose(1, 0, 2).reshape(NDC * DM, P)
                 .astype(ml_dtypes.bfloat16))
    h["w1sc"] = f(np.asarray(I["L1_ffn_w_scale"]).reshape(NDC, P).T)
    h["w1b"] = f(np.asarray(I["L1_ffn_bias"]).reshape(NDC, P).T)
    h["identb"] = np.eye(P, dtype=ml_dtypes.bfloat16)
    h["identf"] = np.eye(P, dtype=np.float32)
    mt = np.zeros((CH, CH), dtype=ml_dtypes.bfloat16)
    for m in range(CH):
        mt[m, m:] = 1.0
    h["maskT"] = mt
    h["onesb"] = np.ones((P, P), dtype=ml_dtypes.bfloat16)
    h["onesf"] = np.ones((1, P), dtype=np.float32)
    cv = np.zeros((P, 4), dtype=np.float32)
    cv[:, 0] = 1e-5
    cv[:, 1] = 1e-7
    h["constv"] = cv
    return h


def build_nc(n_phases):
    nc = bass.Bass("TRN2", target_bir_lowering=False, debug=False)
    es = ExitStack()

    def din(name, shape, dtype=F32):
        return nc.dram_tensor(name, list(shape), dtype, kind="ExternalInput").ap()

    def dint(name, shape, dtype=F32):
        return nc.dram_tensor(name, list(shape), dtype)

    def dsh(name, shape, dtype=F32):
        return nc.dram_tensor(name, list(shape), dtype, addr_space="Shared")

    def sb(name, shape, dtype, stack=None):
        return (stack or es).enter_context(nc.sbuf_tensor(name, list(shape), dtype))

    def psum(name, shape, dtype=F32, stack=None):
        return (stack or es).enter_context(nc.psum_tensor(name, list(shape), dtype))

    x_in = din("x_fm", (DM, TOK))
    maskpre = din("maskpre", (P, NCORES))
    w0q = din("w0q", (NH * DM, DK)); w0k = din("w0k", (NH * DM, DK))
    w0v = din("w0v", (NH * DM, DK))
    fq1T = din("fq1T", (DK, DK)); fq2T = din("fq2T", (DK, DK))
    fk1T = din("fk1T", (DK, DK)); fk2T = din("fk2T", (DK, DK))
    fq1b = din("fq1b", (DK, 1)); fq2b = din("fq2b", (DK, 1))
    fk1b = din("fk1b", (DK, 1)); fk2b = din("fk2b", (DK, 1))
    normw = din("normw", (DK, 1))
    w0o = din("w0o", (NDC * DM, P))
    gwT = din("gwT", (DM, NE))
    wup = din("wup", (NHC * DM, P)); wg = din("wg", (NHC * DM, P))
    wd = din("wd", (NDC * HF, P))
    upA = din("upA", (P, NE * NDC * R)); gA = din("gA", (P, NE * NDC * R))
    upB = din("upB", (NHC * R, NE * P)); gB = din("gB", (NHC * R, NE * P))
    dAp = din("dAp", (NHC * P, 256))
    dBTA = din("dBTA", (P, DM)); dBTB = din("dBTB", (P, DM))
    w1qkv = din("w1qkv", (48 * DM, P)); w1o = din("w1o", (NDC * DM, P))
    w1T = din("w1T", (NDC * DM, P), BF16)
    w1sc = din("w1sc", (P, NDC)); w1b = din("w1b", (P, NDC))
    identb = din("identb", (P, P), BF16)
    identf = din("identf", (P, P), F32)
    maskT = din("maskT", (CH, CH), BF16)
    onesb = din("onesb", (P, P), BF16)
    onesf = din("onesf", (1, P), F32)
    constv = din("constv", (P, 4), F32)
    out = nc.dram_tensor("out", [DM, TOK], F32, kind="ExternalOutput").ap()

    kv_in = dint("kv_in", (NH * DK, DK))
    kv_out = dsh("kv_out", (NCORES * NH * DK, DK))
    gs_in = dint("gs_in", (1, 4))
    gs_out = dsh("gs_out", (NCORES, 4))
    k_in = dint("k_in", (DM, TOK), BF16)
    k_out = dsh("k_out", (NCORES * DM, TOK), BF16)
    km_in = dint("km_in", (1, NH))
    km_out = dsh("km_out", (1, NH))
    cs_in = dint("cs_in", (NCORES * NH, TOK))
    cs_rs = dint("cs_rs", (NH, TOK))
    qd = dint("qd", (DM, TOK), BF16)
    b_ce = dint("b_ce", (NE, TOK), BF16)
    b_negb = dint("b_negb", (1, NH))
    b_qsc = dint("b_qsc", (TOK, 1))
    b_v = dint("b_v", (2 * NE, P), BF16)
    b_rq = dint("b_rq", (P, 16))

    X32 = sb("X32", (P, NDC * TOK), F32)
    XBF = sb("XBF", (P, NDC * TOK), BF16)
    idb_s = sb("idb_s", (P, P), BF16)
    idf_s = sb("idf_s", (P, P), F32)
    mask_s = sb("mask_s", (CH, CH), BF16)
    onesb_s = sb("onesb_s", (P, P), BF16)
    onesf_s = sb("onesf_s", (1, P), F32)
    maskp_s = sb("maskp_s", (P, NCORES), F32)
    fmb = sb("fmb", (P, 8), F32)
    const_s = sb("const_s", (P, 4), F32)

    block_cm = nc.Block()
    block = block_cm.__enter__()
    S = Sched(nc, es, block)

    S.dma("sync", X32[:, :].rearrange("p (c n) -> p c n", n=TOK),
          x_in.rearrange("(c p) n -> p c n", p=P))
    ld_x = S.now("LD")
    S.dma("sync", idb_s[:, :], identb)
    S.dma("sync", idf_s[:, :], identf)
    S.dma("sync", mask_s[:, :], maskT)
    S.dma("sync", onesb_s[:, :], onesb)
    S.dma("sync", onesf_s[:, :], onesf)
    S.dma("sync", maskp_s[:, :], maskpre)
    S.dma("sync", const_s[:, :], constv)
    nc.const_aps.register(F32, 1e-5, const_s[:, 0:1])
    nc.const_aps.register(F32, 1e-7, const_s[:, 1:2])
    ld_consts = S.now("LD")
    S.emit("vector", lambda e: e.tensor_copy(XBF[:, :], X32[:, :]),
           wait=[("LD", ld_x)], inc=[("VE", 1)])


    p1 = ExitStack()
    GA = sb("GA", (P, NDC * TOK), BF16, stack=p1)
    GB = sb("GB", (P, NDC * TOK), BF16, stack=p1)
    kvtot = sb("kvtot", (P, NH * DK), F32, stack=p1)
    rms_s = sb("rms_s", (P, 4), F32, stack=p1)
    sqscr = sb("sqscr", (P, DK), BF16, stack=p1)
    p1a = ExitStack()
    wq_s = sb("wq", (P, NDC * DK), BF16, stack=p1a)
    wk_s = sb("wk", (P, NDC * DK), BF16, stack=p1a)
    wv_s = sb("wv", (P, NDC * DK), BF16, stack=p1a)
    fmw = sb("fmw", (P, 4 * DK), BF16, stack=p1a)
    qraw = sb("qraw", (P, TOK), BF16, stack=p1a)
    kraw = qraw
    qa_s = sb("qa_s", (P, TOK), BF16, stack=p1a)
    k2_s = sb("k2_s", (P, TOK), BF16, stack=p1a)
    vfm_s = sb("vfm_s", (P, TOK), BF16, stack=p1a)
    ktm = sb("ktm", (CH, NCH * DK), BF16, stack=p1a)
    vtm = sb("vtm", (CH, NCH * DK), BF16, stack=p1a)
    S32_s = sb("S32_s", (P, DK), F32, stack=p1a)
    Sbf_s = sb("Sbf_s", (P, DK), BF16, stack=p1a)
    sT_s = sb("sT_s", (CH, CH), BF16, stack=p1a)

    p1ps = ExitStack()
    ps_p = [psum(f"ps_p{i}", (P, 512), stack=p1ps) for i in range(2)]
    ps_o = [psum(f"ps_o{i}", (CH, DK), stack=p1ps) for i in range(2)]
    ps_sT = psum("ps_sT", (CH, CH), stack=p1ps)
    ps_kv = psum("ps_kv", (P, DK), stack=p1ps)
    ps_tr = psum("ps_tr", (P, P), BF16, stack=p1ps)
    ps_fx = psum("ps_fx", (P, DK), stack=p1ps)

    S.dma("gpsimd", fmw[:, 0:DK], fq1T)
    S.dma("gpsimd", fmw[:, DK:2 * DK], fq2T)
    S.dma("gpsimd", fmw[:, 2 * DK:3 * DK], fk1T)
    S.dma("gpsimd", fmw[:, 3 * DK:4 * DK], fk2T)
    wt_fm = S.now("WT")
    S.dma("sync", fmb[:, 0:1], fq1b)
    S.dma("sync", fmb[:, 1:2], fq2b)
    S.dma("sync", fmb[:, 2:3], fk1b)
    S.dma("sync", fmb[:, 3:4], fk2b)
    S.dma("sync", fmb[:, 4:5], normw)
    ld_fm = S.now("LD")
    S.emit("vector", lambda e: e.tensor_scalar(fmb[:, 5:6], fmb[:, 0:1],
                                               ATT_SCALE, None, op0=ALU.mult),
           wait=[("LD", ld_fm)], inc=[("VE", 1)])
    ve_fmb = S.now("VE")

    wt_headw = {}

    def load_head_w(h, which):
        dst, srcw = {"q": (wq_s, w0q), "k": (wk_s, w0k), "v": (wv_s, w0v)}[which]
        S.dma("gpsimd", dst[:, :].rearrange("p (c n) -> p c n", n=DK),
              srcw[h * DM:(h + 1) * DM, :].rearrange("(c p) n -> p c n", p=P),
              wait=[("TE", S.now("TE"))])
        wt_headw[(h, which)] = S.now("WT")

    for w_ in ("q", "k", "v"):
        load_head_w(0, w_)

    proj_evac = [0, 0]
    fm_se = [0]
    fm_ve = [S.now("VE")]
    tr_ve = [0]
    kvtot_ve = [0]
    sT_ve = [0]
    s32_ve = [0]
    sbf_ve = [0]
    o_se = [0, 0]

    def p1_proj(h, which, dst):
        ws = {"q": wq_s, "k": wk_s, "v": wv_s}[which]
        wtw = wt_headw[(h, which)]
        for tt in range(2):
            pb = ps_p[tt]
            for c in range(NDC):
                S.emit("tensor", lambda e, ws=ws, c=c, tt=tt, pb=pb:
                       e.matmul(pb[:, :], ws[:, c * DK:(c + 1) * DK],
                                XBF[:, c * TOK + tt * 512:c * TOK + tt * 512 + 512],
                                start=(c == 0), stop=(c == NDC - 1)),
                       wait=([("WT", wtw), ("SE", proj_evac[tt]),
                              ("VE", fm_ve[0]), ("SE", fm_se[0])]
                             if c == 0 else []),
                       inc=[("TE", 1)])
            te_mm = S.now("TE")
            S.emit("scalar", lambda e, dst=dst, tt=tt, pb=pb:
                   e.activation(out=dst[:, tt * 512:tt * 512 + 512],
                                in_=pb[:, :], func=AF.Copy),
                   wait=[("TE", te_mm)], inc=[("SE", 1)])
            proj_evac[tt] = S.now("SE")
        if h + 1 < NH:
            load_head_w(h + 1, which)

    def p1_fmap(h, src, w0c, w1c, b0c, b1c, k_is):
        for tt in range(2):
                pb0, pb1 = ps_p[0], ps_p[1]
                S.emit("tensor", lambda e, src=src, w0c=w0c, tt=tt, pb0=pb0:
                       e.matmul(pb0[:, :], fmw[:, w0c * DK:(w0c + 1) * DK],
                                src[:, tt * 512:tt * 512 + 512],
                                start=True, stop=True),
                       wait=[("SE", proj_evac[0]), ("SE", proj_evac[1]),
                             ("SE", fm_se[0]), ("VE", fm_ve[0])],
                       inc=[("TE", 1)])
                S.emit("tensor", lambda e, src=src, w1c=w1c, tt=tt, pb1=pb1:
                       e.matmul(pb1[:, :], fmw[:, w1c * DK:(w1c + 1) * DK],
                                src[:, tt * 512:tt * 512 + 512],
                                start=True, stop=True),
                       inc=[("TE", 1)])
                te_fm = S.now("TE")
                S.emit("scalar", lambda e, tt=tt, b0c=b0c, k_is=k_is, pb0=pb0:
                       e.activation(out=qa_s[:, tt * 512:tt * 512 + 512],
                                    in_=pb0[:, :], func=AF.Identity,
                                    bias=fmb[:, b0c:b0c + 1],
                                    scale=(1.0 if k_is else ATT_SCALE)),
                       wait=[("TE", te_fm), ("VE", ve_fmb), ("VE", fm_ve[0])],
                       inc=[("SE", 1)])
                fm_se[0] = S.now("SE")
                dsl = (k2_s[:, tt * 512:tt * 512 + 512] if k_is else
                       GB[:, h * TOK + tt * 512: h * TOK + tt * 512 + 512])
                S.emit("vector", lambda e, tt=tt, b1c=b1c, dsl=dsl, pb1=pb1:
                       e.scalar_tensor_tensor(dsl, pb1[:, :],
                                              fmb[:, b1c:b1c + 1],
                                              qa_s[:, tt * 512:tt * 512 + 512],
                                              op0=ALU.add, op1=ALU.mult),
                       wait=[("SE", fm_se[0])], inc=[("VE", 1)])
                fm_ve[0] = S.now("VE")

    def p1_head(h):
        p1_proj(h, "q", qraw)
        p1_fmap(h, qraw, 0, 1, 5, 1, False)
        p1_proj(h, "k", kraw)
        p1_fmap(h, kraw, 2, 3, 2, 3, True)
        p1_proj(h, "v", vfm_s)
        ve_fmaps = S.now("VE")
        for (src, dst) in ((k2_s, ktm), (vfm_s, vtm)):
            for g in range(8):
                S.emit("tensor", lambda e, src=src, g=g:
                       e.transpose(ps_tr[:, :], src[:, g * P:(g + 1) * P],
                                   idb_s[:, :]),
                       wait=[("VE", ve_fmaps), ("VE", tr_ve[0]),
                             ("SE", proj_evac[0]), ("SE", proj_evac[1])],
                       inc=[("TE", 1)])
                te_t = S.now("TE")
                S.emit("vector", lambda e, dst=dst, g=g:
                       e.tensor_copy(dst[:, (2 * g) * DK:(2 * g) * DK + DK],
                                     ps_tr[0:CH, :]),
                       wait=[("TE", te_t)], inc=[("VE", 1)])
                S.emit("vector", lambda e, dst=dst, g=g:
                       e.tensor_copy(dst[:, (2 * g + 1) * DK:(2 * g + 1) * DK + DK],
                                     ps_tr[CH:P, :]),
                       inc=[("VE", 1)])
                tr_ve[0] = S.now("VE")
        ve_tm = S.now("VE")
        for n in range(NCH):
            S.emit("tensor", lambda e, n=n:
                   e.matmul(ps_fx[:, :], ktm[:, n * DK:(n + 1) * DK],
                            vtm[:, n * DK:(n + 1) * DK],
                            start=(n == 0), stop=(n == NCH - 1)),
                   wait=([("VE", ve_tm), ("VE", kvtot_ve[0])] if n == 0 else []),
                   inc=[("TE", 1)])
        te_T = S.now("TE")
        S.emit("vector", lambda e, h=h:
               e.tensor_copy(kvtot[:, h * DK:(h + 1) * DK], ps_fx[:, :]),
               wait=[("TE", te_T)], inc=[("VE", 1)])
        kvtot_ve[0] = S.now("VE")
        S.emit("vector", lambda e: e.memset(S32_s[:, :], 0.0),
               wait=[("TE", te_T)], inc=[("VE", 1)])
        S.emit("vector", lambda e: e.memset(Sbf_s[:, :], 0.0), inc=[("VE", 1)])
        sbf_ve[0] = S.now("VE")
        s32_ve[0] = S.now("VE")
        for n in range(NCH):
            ob = ps_o[n % 2]
            if n > 0:
                S.emit("tensor", lambda e, n=n, ob=ob, h=h:
                       e.matmul(ob[:, :],
                                GB[:, h * TOK + n * CH: h * TOK + n * CH + CH],
                                Sbf_s[:, :], start=True, stop=False),
                       wait=[("VE", sbf_ve[0]), ("SE", o_se[n % 2])],
                       inc=[("TE", 1)])
            S.emit("tensor", lambda e, n=n, h=h:
                   e.matmul(ps_sT[:, :],
                            k2_s[:, n * CH:(n + 1) * CH],
                            GB[:, h * TOK + n * CH: h * TOK + n * CH + CH],
                            start=True, stop=True),
                   wait=([("SE", o_se[0]), ("SE", o_se[1]), ("VE", sT_ve[0])]
                         if n == 0 else [("VE", sT_ve[0])]),
                   inc=[("TE", 1)])
            te_sT = S.now("TE")
            S.emit("vector", lambda e: e.tensor_mul(sT_s[:, :], ps_sT[0:CH, 0:CH],
                                                    mask_s[:, :]),
                   wait=[("TE", te_sT)], inc=[("VE", 1)])
            sT_ve[0] = S.now("VE")
            S.emit("tensor", lambda e, n=n, ob=ob:
                   e.matmul(ob[:, :], sT_s[:, :],
                            vtm[:, n * DK:(n + 1) * DK],
                            start=(n == 0), stop=True),
                   wait=[("VE", sT_ve[0])], inc=[("TE", 1)])
            te_o = S.now("TE")
            S.emit("tensor", lambda e, n=n:
                   e.matmul(ps_kv[:, :], ktm[:, n * DK:(n + 1) * DK],
                            vtm[:, n * DK:(n + 1) * DK],
                            start=True, stop=True),
                   wait=[("VE", s32_ve[0])], inc=[("TE", 1)])
            te_kv = S.now("TE")
            S.emit("vector", lambda e: e.tensor_add(S32_s[:, :], S32_s[:, :],
                                                    ps_kv[:, :]),
                   wait=[("TE", te_kv)], inc=[("VE", 1)])
            s32_ve[0] = S.now("VE")
            S.emit("vector", lambda e: e.tensor_copy(Sbf_s[:, :], S32_s[:, :]),
                   inc=[("VE", 1)])
            sbf_ve[0] = S.now("VE")
            S.emit("scalar", lambda e, n=n, h=h, ob=ob:
                   e.activation(out=GA[(n % 2) * CH:(n % 2) * CH + CH,
                                       h * TOK + (n // 2) * DK:
                                       h * TOK + (n // 2) * DK + DK],
                                in_=ob[:, :], func=AF.Copy),
                   wait=[("TE", te_o)], inc=[("SE", 1)])
            o_se[n % 2] = S.now("SE")

    for h in range(NH):
        p1_head(h)

    S.flush()
    S.barrier()
    p1a.close()
    ve_kv_all = S.now("VE")
    S.dma("gpsimd", kv_in.ap().rearrange("(h p) e -> p h e", p=P),
          kvtot[:, :].rearrange("p (h e) -> p h e", e=DK),
          wait=[("VE", ve_kv_all)])
    wt_kvin = S.now("WT")
    cc_kv = S.collective(lambda e: e.collective_compute(
        "AllGather", ALU.bypass, replica_groups=[list(range(NCORES))],
        ins=[kv_in.ap().opt()], outs=[kv_out.ap().opt()]),
        wait=[("WT", wt_kvin)])
    S.emit("vector", lambda e: e.memset(kvtot[:, :], 0.0),
           wait=[("WT", wt_kvin)], inc=[("VE", 1)])
    agb_s = sb("agb", (P, NH * DK), F32, stack=p1)
    ag_ve = [0]
    for r in range(NCORES):
        S.dma("sync", agb_s[:, :].rearrange("p (h e) -> p h e", e=DK),
              kv_out[r * NH * DK:(r + 1) * NH * DK, :]
              .rearrange("(h p) e -> p h e", p=P),
              wait=[("CC", cc_kv), ("VE", ag_ve[0])])
        ld_r = S.now("LD")
        S.emit("vector", lambda e, r=r:
               e.scalar_tensor_tensor(kvtot[:, :], agb_s[:, :],
                                      maskp_s[:, r:r + 1], kvtot[:, :],
                                      op0=ALU.mult, op1=ALU.add),
               wait=[("LD", ld_r)], inc=[("VE", 1)])
        ag_ve[0] = S.now("VE")
    ve_sall = S.now("VE")

    Sg_bf = sb("Sg_bf", (P, DK), BF16, stack=p1)
    fx_se = [0]
    for h in range(NH):
        S.emit("vector", lambda e, h=h:
               e.tensor_copy(Sg_bf[:, :], kvtot[:, h * DK:(h + 1) * DK]),
               wait=[("VE", ve_sall), ("TE", S.now("TE"))], inc=[("VE", 1)])
        ve_sg = S.now("VE")
        for g in range(8):
            S.emit("tensor", lambda e, h=h, g=g:
                   e.matmul(ps_fx[:, :],
                            GB[:, h * TOK + g * P: h * TOK + g * P + P],
                            Sg_bf[:, :], start=True, stop=True),
                   wait=[("VE", ve_sg), ("VE", tr_ve[0])], inc=[("TE", 1)])
            te_f = S.now("TE")
            osl = GA[:, h * TOK + g * DK: h * TOK + g * DK + DK]
            S.emit("vector", lambda e, osl=osl:
                   e.tensor_add(osl, osl, ps_fx[:, :]),
                   wait=[("TE", te_f)], inc=[("VE", 1)])
            ve_fx = S.now("VE")
            S.emit("scalar", lambda e, osl=osl:
                   e.activation(out=sqscr[:, :], in_=osl, func=AF.Square,
                                accum_out=rms_s[:, 0:1]),
                   wait=[("VE", ve_fx), ("VE", tr_ve[0])], inc=[("SE", 1)])
            se_sq2 = S.now("SE")
            S.emit("scalar", lambda e:
                   e.activation(out=rms_s[:, 1:2], in_=rms_s[:, 0:1],
                                func=AF.Sqrt, scale=1.0 / DK, bias=RMSEPS),
                   wait=[("SE", se_sq2)], inc=[("SE", 1)])
            se_rms = S.now("SE")
            S.emit("vector", lambda e: e.reciprocal(rms_s[:, 2:3], rms_s[:, 1:2]),
                   wait=[("SE", se_rms)], inc=[("VE", 1)], vspecial=True)
            ve_rcp = S.now("VE")
            S.emit("vector", lambda e, osl=osl:
                   e.tensor_scalar(osl, osl, rms_s[:, 2:3], None, op0=ALU.mult),
                   wait=[("VE", ve_rcp)], inc=[("VE", 1)])
            ve_on = S.now("VE")
            S.emit("tensor", lambda e, osl=osl:
                   e.transpose(ps_tr[:, :], osl, idb_s[:, :]),
                   wait=[("VE", ve_on)], inc=[("TE", 1)])
            te_tr2 = S.now("TE")
            S.emit("vector", lambda e, h=h, g=g:
                   e.tensor_scalar(GB[:, h * TOK + g * DK: h * TOK + g * DK + DK],
                                   ps_tr[:, :], fmb[:, 4:5], None, op0=ALU.mult),
                   wait=[("TE", te_tr2)], inc=[("VE", 1)])
            tr_ve[0] = S.now("VE")
    ve_ofm = S.now("VE")

    woj_s = [sb(f"woj{i}", (P, NDC * P), BF16, stack=p1) for i in range(2)]
    wo_free = [0, 0]
    wo_evac = [0, 0]
    wo_wt = [0] * (NDC + 2)

    def load_wo(j):
        b = j % 2
        S.dma("gpsimd", woj_s[b][:, :].rearrange("p (c n) -> p c n", n=P),
              w0o[j * DM:(j + 1) * DM, :].rearrange("(c p) n -> p c n", p=P),
              wait=[("TE", wo_free[b])])
        return S.now("WT")

    wo_wt[0] = load_wo(0)
    wo_wt[1] = load_wo(1)
    for j in range(NDC):
        b = j % 2
        for tt in range(2):
            pb = ps_p[tt]
            for c in range(NDC):
                S.emit("tensor", lambda e, b=b, c=c, tt=tt, pb=pb:
                       e.matmul(pb[:, :], woj_s[b][:, c * P:(c + 1) * P],
                                GB[:, c * TOK + tt * 512: c * TOK + tt * 512 + 512],
                                start=(c == 0), stop=(c == NDC - 1)),
                       wait=([("WT", wo_wt[j]), ("VE", ve_ofm),
                              ("VE", wo_evac[tt]), ("SE", proj_evac[tt]),
                              ("SE", fm_se[0])]
                             if c == 0 else []),
                       inc=[("TE", 1)])
            te_wo = S.now("TE")
            S.emit("vector", lambda e, j=j, tt=tt, pb=pb:
                   e.tensor_add(X32[:, j * TOK + tt * 512: j * TOK + tt * 512 + 512],
                                X32[:, j * TOK + tt * 512: j * TOK + tt * 512 + 512],
                                pb[:, :]),
                   wait=[("TE", te_wo)], inc=[("VE", 1)])
            wo_evac[tt] = S.now("VE")
        wo_free[b] = S.now("TE")
        if j + 2 < NDC:
            wo_wt[j + 2] = load_wo(j + 2)

    S.flush()
    S.barrier()
    p1ps.close()

    ln_ctr = [0]

    def layer_norm(arena):
        ln_ctr[0] += 1
        sfx = str(ln_ctr[0])
        st = ExitStack()
        ps1 = psum("lnp1" + sfx, (1, TOK), stack=st)
        ps2 = psum("lnp2" + sfx, (1, TOK), stack=st)
        mub = psum("lnmub" + sfx, (P, TOK), stack=st)
        rsb = psum("lnrsb" + sfx, (P, TOK), stack=st)
        mu = sb("lnmu" + sfx, (1, TOK), F32, stack=st)
        e2 = sb("lne2" + sfx, (1, TOK), F32, stack=st)
        sd = sb("lnsd" + sfx, (1, TOK), F32, stack=st)
        rs = sb("lnrs" + sfx, (1, TOK), F32, stack=st)
        S.barrier()
        S.emit("vector", lambda e: e.tensor_copy(XBF[:, :], X32[:, :]),
               inc=[("VE", 1)])
        v_cp = S.now("VE")
        S.emit("scalar", lambda e: e.activation(out=arena[:, 0:NDC * TOK],
                                                in_=XBF[:, :], func=AF.Square),
               wait=[("VE", v_cp)], inc=[("SE", 1)])
        se_sq = S.now("SE")
        for th in range(2):
            tsl = slice(th * 512, th * 512 + 512)
            for c in range(NDC):
                csl = slice(c * TOK + th * 512, c * TOK + th * 512 + 512)
                S.emit("tensor", lambda e, csl=csl, c=c, tsl=tsl:
                       e.matmul(ps1[0:1, tsl], onesb_s[:, 0:1], XBF[:, csl],
                                start=(c == 0), stop=(c == NDC - 1)),
                       wait=([("VE", v_cp)] if c == 0 else []), inc=[("TE", 1)])
            for c in range(NDC):
                csl = slice(c * TOK + th * 512, c * TOK + th * 512 + 512)
                S.emit("tensor", lambda e, csl=csl, c=c, tsl=tsl:
                       e.matmul(ps2[0:1, tsl], onesb_s[:, 0:1], arena[:, csl],
                                start=(c == 0), stop=(c == NDC - 1)),
                       wait=([("SE", se_sq)] if c == 0 else []), inc=[("TE", 1)])
        te_sums = S.now("TE")
        S.vserial = True
        S.emit("vector", lambda e: e.tensor_scalar(mu[0:1, :], ps1[0:1, :],
                                                   1.0 / DM, None, op0=ALU.mult),
               wait=[("TE", te_sums)], inc=[("VE", 1)])
        S.emit("vector", lambda e: e.tensor_scalar(e2[0:1, :], ps2[0:1, :],
                                                   1.0 / DM, None, op0=ALU.mult),
               inc=[("VE", 1)])
        S.emit("vector", lambda e: e.tensor_scalar(sd[0:1, :], mu[0:1, :],
                                                   -1.0, None, op0=ALU.mult),
               inc=[("VE", 1)])
        S.emit("vector", lambda e: e.tensor_mul(sd[0:1, :], sd[0:1, :], mu[0:1, :]),
               inc=[("VE", 1)])
        S.emit("vector", lambda e: e.tensor_add(e2[0:1, :], e2[0:1, :], sd[0:1, :]),
               inc=[("VE", 1)])
        v_var = S.now("VE")
        S.emit("scalar", lambda e: e.activation(out=sd[0:1, :], in_=e2[0:1, :],
                                                func=AF.Sqrt, bias=LNEPS),
               wait=[("VE", v_var)], inc=[("SE", 1)])
        se_sd = S.now("SE")
        S.emit("vector", lambda e: e.reciprocal(rs[0:1, :], sd[0:1, :]),
               wait=[("SE", se_sd)], inc=[("VE", 1)], vspecial=True)
        v_rs = S.now("VE")
        S.vserial = False
        for th in range(2):
            tsl = slice(th * 512, th * 512 + 512)
            S.emit("tensor", lambda e, tsl=tsl:
                   e.matmul(mub[:, tsl], onesf_s[0:1, :], mu[0:1, tsl],
                            start=True, stop=True),
                   wait=[("VE", v_rs)], inc=[("TE", 1)])
            S.emit("tensor", lambda e, tsl=tsl:
                   e.matmul(rsb[:, tsl], onesf_s[0:1, :], rs[0:1, tsl],
                            start=True, stop=True),
                   inc=[("TE", 1)])
        te_bc = S.now("TE")
        for c in range(NDC):
            for th in range(2):
                csl = slice(c * TOK + th * 512, c * TOK + th * 512 + 512)
                tsl = slice(th * 512, th * 512 + 512)
                S.emit("vector", lambda e, csl=csl, tsl=tsl:
                       e.tensor_sub(X32[:, csl], X32[:, csl], mub[:, tsl]),
                       wait=([("TE", te_bc)] if (c == 0 and th == 0) else []),
                       inc=[("VE", 1)])
                S.emit("vector", lambda e, csl=csl, tsl=tsl:
                       e.tensor_mul(X32[:, csl], X32[:, csl], rsb[:, tsl]),
                       inc=[("VE", 1)])
        S.emit("vector", lambda e: e.tensor_copy(XBF[:, :], X32[:, :]),
               inc=[("VE", 1)])
        S.flush()
        st.close()

    if not DBG_SKIP_LN:
        layer_norm(GA)
        S.barrier()
        S.flush()
        p1.close()
    else:
        dbg_gb = nc.dram_tensor("dbg_gb", [P, NDC * TOK], BF16,
                                kind="ExternalOutput").ap()
        dbg_ga = nc.dram_tensor("dbg_ga", [P, NDC * TOK], BF16,
                                kind="ExternalOutput").ap()
        S.barrier()
        S.dma("sync", dbg_gb, GB[:, :], sem="ST")
        S.dma("sync", dbg_ga, GA[:, :], sem="ST")


    if (not DBG_SKIP_LN) and n_phases >= 2:
        S.barrier()
        HHC = NHC // 2

        p2g = ExitStack()
        gwT_s = sb("gwT_s", (P, NDC * NE), BF16, stack=p2g)
        lgb = sb("lgb", (NE, TOK), BF16, stack=p2g)
        lgtm = sb("lgtm", (P, NE * NE), F32, stack=p2g)
        v1s = sb("v1s", (P, NE), F32, stack=p2g)
        v2s = sb("v2s", (P, NE), F32, stack=p2g)
        w1s = sb("w1s", (P, NE), F32, stack=p2g)
        w2s = sb("w2s", (P, NE), F32, stack=p2g)
        vvb = sb("vvb", (P, 2 * NE), BF16, stack=p2g)
        vvT = sb("vvT", (2 * NE, P), BF16, stack=p2g)
        v1r = sb("v1r", (1, TOK), BF16, stack=p2g)
        v2r = sb("v2r", (1, TOK), BF16, stack=p2g)
        erow = sb("erow", (1, TOK), F32, stack=p2g)
        w1r = sb("w1r", (1, TOK), F32, stack=p2g)
        wrb = sb("wrb", (1, TOK), BF16, stack=p2g)
        gsc = sb("gsc", (1, 16), F32, stack=p2g)
        gsr = sb("gsr", (1, 4 * NCORES), F32, stack=p2g)
        cst = sb("cst", (P, NE * NE), F32, stack=p2g)
        ceb = sb("ceb", (P, NE * NE), BF16, stack=p2g)
        ceT = sb("ceT", (NE, NE * P), BF16, stack=p2g)
        scrA = sb("scrA", (P, NE), F32, stack=p2g)
        scrB = sb("scrB", (P, NE), F32, stack=p2g)
        ps_g = psum("ps_g", (NE, TOK), stack=p2g)
        ps_tt2 = psum("ps_tt2", (P, P), BF16, stack=p2g)

        S.vserial = True
        S.dma("gpsimd", gwT_s[:, :].rearrange("p (c e) -> p c e", e=NE),
              gwT.rearrange("(c p) e -> p c e", p=P))
        wt_gw = S.now("WT")
        ve_base0 = S.now("VE")
        for th in range(2):
            for c in range(NDC):
                S.emit("tensor", lambda e, c=c, th=th:
                       e.matmul(ps_g[:, th * 512:th * 512 + 512],
                                gwT_s[:, c * NE:(c + 1) * NE],
                                XBF[:, c * TOK + th * 512:c * TOK + th * 512 + 512],
                                start=(c == 0), stop=(c == NDC - 1)),
                       wait=([("WT", wt_gw), ("VE", ve_base0)] if c == 0 else []),
                       inc=[("TE", 1)])
        te_lg = S.now("TE")
        S.emit("vector", lambda e: e.tensor_copy(lgb[:, :], ps_g[:, :]),
               wait=[("TE", te_lg)], inc=[("VE", 1)])
        ve_lgb = S.now("VE")
        for g in range(NE):
            S.emit("tensor", lambda e, g=g:
                   e.transpose(ps_tt2[:, 0:NE], lgb[:, g * P:(g + 1) * P], idb_s[0:NE, 0:NE]),
                   wait=[("VE", S.now("VE"))], inc=[("TE", 1)])
            te_t = S.now("TE")
            S.emit("vector", lambda e, g=g:
                   e.tensor_copy(lgtm[:, g * NE:(g + 1) * NE], ps_tt2[:, 0:NE]),
                   wait=[("TE", te_t)], inc=[("VE", 1)])
        for g in range(NE):
            lg_g = lgtm[:, g * NE:(g + 1) * NE]
            S.emit("vector", lambda e, lg_g=lg_g, g=g:
                   e.reduce_max(v1s[:, g:g + 1], lg_g, axis=AX.X), inc=[("VE", 1)], vspecial=True)
            S.emit("vector", lambda e, g=g, lg_g=lg_g:
                   e.tensor_scalar(scrA[:, :], lg_g, v1s[:, g:g + 1], None,
                                   op0=ALU.is_equal), inc=[("VE", 1)])
            S.emit("vector", lambda e, g=g, lg_g=lg_g:
                   e.scalar_tensor_tensor(scrB[:, :], scrA[:, :], -BIGNEG, lg_g,
                                          op0=ALU.mult, op1=ALU.add), inc=[("VE", 1)])
            S.emit("vector", lambda e, g=g:
                   e.reduce_max(v2s[:, g:g + 1], scrB[:, :], axis=AX.X), inc=[("VE", 1)], vspecial=True)
        S.emit("vector", lambda e: e.tensor_copy(vvb[:, 0:NE], v1s[:, :]), inc=[("VE", 1)])
        S.emit("vector", lambda e: e.tensor_copy(vvb[:, NE:2 * NE], v2s[:, :]),
               inc=[("VE", 1)])
        ve_vv = S.now("VE")
        S.emit("tensor", lambda e: e.transpose(ps_tt2[0:2 * NE, :], vvb[:, :],
                                               idb_s[:, :]),
               wait=[("VE", ve_vv)], inc=[("TE", 1)])
        te_vvt = S.now("TE")
        S.emit("vector", lambda e: e.tensor_copy(vvT[:, :], ps_tt2[0:2 * NE, :]),
               wait=[("TE", te_vvt)], inc=[("VE", 1)])
        ve_vvT = S.now("VE")
        S.dma("sync", b_v.ap(), vvT[:, :], wait=[("VE", ve_vvT)])
        ld_bv = S.now("LD")
        S.dma("sync", v1r[0:1, :].rearrange("o (g p) -> o g p", p=P),
              b_v[0:NE, :].rearrange("(o g) p -> o g p", o=1), wait=[("LD", ld_bv)])
        S.dma("sync", v2r[0:1, :].rearrange("o (g p) -> o g p", p=P),
              b_v[NE:2 * NE, :].rearrange("(o g) p -> o g p", o=1))
        ld_vr = S.now("LD")
        S.emit("vector", lambda e: e.reduce_max(gsc[0:1, 0:1], v1r[0:1, :], axis=AX.X),
               wait=[("LD", ld_vr)], inc=[("VE", 1)], vspecial=True)
        S.emit("vector", lambda e: e.reduce_max(gsc[0:1, 1:2], v2r[0:1, :], axis=AX.X),
               inc=[("VE", 1)], vspecial=True)
        S.emit("vector", lambda e: e.tensor_scalar(gsc[0:1, 4:6], gsc[0:1, 0:2],
                                                   -1.0, None, op0=ALU.mult),
               inc=[("VE", 1)])
        ve_ml = S.now("VE")
        S.emit("scalar", lambda e: e.activation(out=erow[0:1, :], in_=v1r[0:1, :],
                                                func=AF.Exp, bias=gsc[0:1, 4:5],
                                                accum_out=gsc[0:1, 2:3]),
               wait=[("VE", ve_ml)], inc=[("SE", 1)])
        se_e1 = S.now("SE")
        S.emit("scalar", lambda e: e.activation(out=erow[0:1, :], in_=v2r[0:1, :],
                                                func=AF.Exp, bias=gsc[0:1, 5:6],
                                                accum_out=gsc[0:1, 3:4]),
               wait=[("SE", se_e1)], inc=[("SE", 1)])
        se_e2 = S.now("SE")
        S.dma("sync", gs_in.ap(), gsc[0:1, 0:4], wait=[("SE", se_e2)])
        ld_gs = S.now("LD")
        cc_gs = S.collective(lambda e: e.collective_compute(
            "AllGather", ALU.bypass, replica_groups=[list(range(NCORES))],
            ins=[gs_in.ap().opt()], outs=[gs_out.ap().opt()]),
            wait=[("LD", ld_gs)])
        for j4 in range(4):
            S.emit("sync", lambda e, j4=j4: e.dma_start(
                out=gsr[0:1, 8 * j4:8 * j4 + 8].rearrange(
                    "o (c s) -> o c s", s=1),
                in_=gs_out[:, j4:j4 + 1].rearrange("(o c) s -> o c s", o=1),
                allow_slow_non_contiguous=True),
                wait=([("CC", cc_gs)] if j4 == 0 else []), inc=[("LD", 16)])
        ld_gss = S.now("LD")
        S.emit("vector", lambda e: e.reduce_max(gsc[0:1, 6:7], gsr[0:1, 0:8],
                                                axis=AX.X),
               wait=[("LD", ld_gss)], inc=[("VE", 1)], vspecial=True)
        S.emit("vector", lambda e: e.reduce_max(gsc[0:1, 7:8], gsr[0:1, 8:16],
                                                axis=AX.X),
               inc=[("VE", 1)], vspecial=True)
        S.emit("vector", lambda e: e.tensor_scalar(gsc[0:1, 8:10], gsc[0:1, 6:8],
                                                   -1.0, None, op0=ALU.mult),
               inc=[("VE", 1)])
        ve_mg = S.now("VE")
        for (row, mcol, zcol) in ((0, 8, 10), (1, 9, 11)):
            S.emit("scalar", lambda e, row=row, mcol=mcol:
                   e.activation(out=erow[0:1, 0:NCORES],
                                in_=gsr[0:1, 8 * row:8 * row + 8],
                                func=AF.Exp, bias=gsc[0:1, mcol:mcol + 1]),
                   wait=[("VE", ve_mg), ("VE", S.now("VE"))], inc=[("SE", 1)])
            se_x = S.now("SE")
            S.emit("vector", lambda e, row=row:
                   e.tensor_mul(erow[0:1, 0:NCORES], erow[0:1, 0:NCORES],
                                gsr[0:1, 8 * (row + 2):8 * (row + 2) + 8]),
                   wait=[("SE", se_x)], inc=[("VE", 1)])
            S.emit("vector", lambda e, zcol=zcol:
                   e.reduce_sum(gsc[0:1, zcol:zcol + 1], erow[0:1, 0:NCORES],
                                axis=AX.X), inc=[("VE", 1)], vspecial=True)
            S.emit("vector", lambda e, zcol=zcol:
                   e.reciprocal(gsc[0:1, zcol + 2:zcol + 3],
                                gsc[0:1, zcol:zcol + 1]), inc=[("VE", 1)], vspecial=True)
        ve_rz = S.now("VE")
        for (vr, ws, mcol, rzcol) in ((v1r, w1s, 8, 12), (v2r, w2s, 9, 13)):
            S.emit("scalar", lambda e, vr=vr, mcol=mcol:
                   e.activation(out=w1r[0:1, :], in_=vr[0:1, :], func=AF.Exp,
                                bias=gsc[0:1, mcol:mcol + 1]),
                   wait=[("VE", ve_rz), ("VE", S.now("VE")),
                         ("TE", S.now("TE"))], inc=[("SE", 1)])
            se_w = S.now("SE")
            S.emit("vector", lambda e, rzcol=rzcol:
                   e.tensor_scalar(w1r[0:1, :], w1r[0:1, :],
                                   gsc[0:1, rzcol:rzcol + 1], None, op0=ALU.mult),
                   wait=[("SE", se_w)], inc=[("VE", 1)])
            S.emit("vector", lambda e: e.tensor_copy(wrb[0:1, :], w1r[0:1, :]),
                   wait=[("TE", S.now("TE"))], inc=[("VE", 1)])
            ve_wrb = S.now("VE")
            for g in range(NE):
                S.emit("tensor", lambda e, g=g:
                       e.transpose(ps_tt2[:, 0:1], wrb[0:1, g * P:(g + 1) * P],
                                   idb_s[0:1, 0:1]),
                       wait=[("VE", ve_wrb), ("VE", S.now("VE"))],
                       inc=[("TE", 1)])
                te_w = S.now("TE")
                S.emit("vector", lambda e, ws=ws, g=g:
                       e.tensor_copy(ws[:, g:g + 1], ps_tt2[:, 0:1]),
                       wait=[("TE", te_w)], inc=[("VE", 1)])
        for g in range(NE):
            lg_g = lgtm[:, g * NE:(g + 1) * NE]
            S.emit("vector", lambda e, lg_g=lg_g, g=g:
                   e.tensor_scalar(scrA[:, :], lg_g, v1s[:, g:g + 1], None,
                                   op0=ALU.is_equal), inc=[("VE", 1)])
            S.emit("vector", lambda e, g=g:
                   e.tensor_scalar(scrA[:, :], scrA[:, :], w1s[:, g:g + 1], None,
                                   op0=ALU.mult), inc=[("VE", 1)])
            S.emit("vector", lambda e, lg_g=lg_g, g=g:
                   e.tensor_scalar(scrB[:, :], lg_g, v2s[:, g:g + 1], None,
                                   op0=ALU.is_equal), inc=[("VE", 1)])
            S.emit("vector", lambda e, g=g:
                   e.tensor_scalar(scrB[:, :], scrB[:, :], w2s[:, g:g + 1], None,
                                   op0=ALU.mult), inc=[("VE", 1)])
            S.emit("vector", lambda e, g=g:
                   e.tensor_add(cst[:, g * NE:(g + 1) * NE], scrA[:, :], scrB[:, :]),
                   inc=[("VE", 1)])
        S.emit("vector", lambda e: e.tensor_copy(ceb[:, :], cst[:, :]), inc=[("VE", 1)])
        ve_ceb = S.now("VE")
        for g in range(NE):
            S.emit("tensor", lambda e, g=g:
                   e.transpose(ps_tt2[0:NE, :], ceb[:, g * NE:(g + 1) * NE],
                               idb_s[:, :]),
                   wait=[("VE", S.now("VE"))], inc=[("TE", 1)])
            te_c = S.now("TE")
            S.emit("vector", lambda e, g=g:
                   e.tensor_copy(ceT[:, g * P:(g + 1) * P], ps_tt2[0:NE, 0:P]),
                   wait=[("TE", te_c)], inc=[("VE", 1)])
        ve_ceT = S.now("VE")
        S.dma("sync", b_ce.ap(), ceT[:, :].rearrange("e (g p) -> e g p", p=P),
              wait=[("VE", ve_ceT)])
        ld_bce = S.now("LD")
        if DBG_SKIP_LN2:
            dbg_g = nc.dram_tensor("dbg_g", [1, 64], F32,
                                   kind="ExternalOutput").ap()
            dbg_t = nc.dram_tensor("dbg_t", [P, 88], F32,
                                   kind="ExternalOutput").ap()
            S.barrier()
            S.dma("sync", dbg_g[0:1, 0:32], gsr[0:1, :], sem="ST")
            S.dma("sync", dbg_g[0:1, 32:48], gsc[0:1, :], sem="ST")
            S.dma("sync", dbg_g[0:1, 48:56], w1s[0:1, :], sem="ST")
            S.dma("sync", dbg_g[0:1, 56:64], w2s[0:1, :], sem="ST")
            S.dma("sync", dbg_t[:, 0:64], lgtm[:, :], sem="ST")
            S.dma("sync", dbg_t[:, 64:72], v1s[:, :], sem="ST")
            S.dma("sync", dbg_t[:, 72:80], v2s[:, :], sem="ST")
            S.dma("sync", dbg_t[:, 80:88], w1s[:, :], sem="ST")
            dbg_r = nc.dram_tensor("dbg_r", [1, 2 * TOK], F32,
                                   kind="ExternalOutput").ap()
            S.emit("vector", lambda e: e.tensor_copy(erow[0:1, :], v1r[0:1, :]),
                   inc=[("VE", 1)])
            ve_dr = S.now("VE")
            S.dma("sync", dbg_r[0:1, 0:TOK], erow[0:1, :], sem="ST",
                  wait=[("VE", ve_dr)])
            S.dma("sync", dbg_r[0:1, TOK:2 * TOK], w1r[0:1, :], sem="ST")
        S.vserial = False
        S.barrier()
        S.flush()
        p2g.close()

        p2m = ExitStack()
        mixed = sb("mixed", (P, HHC * 512), BF16, stack=p2m)
        xau = sb("xau", (R, NE * 512), BF16, stack=p2m)
        xag = sb("xag", (R, NE * 512), BF16, stack=p2m)
        c_bcs = sb("c_bcs", (P, NE * 512), BF16, stack=p2m)
        ce_row = sb("ce_row", (1, NE * 512), BF16, stack=p2m)
        ya_s = sb("ya_s", (P, 2 * 512), BF16, stack=p2m)
        dbTA_s = sb("dbTA_s", (P, NDC * P), BF16, stack=p2m)
        dbTB_s = sb("dbTB_s", (P, NDC * P), BF16, stack=p2m)
        upa_s = sb("upa_s", (P, NE * NDC * R), BF16, stack=p2m)
        ga_s = sb("ga_s", (P, NE * NDC * R), BF16, stack=p2m)
        bu_s = sb("bu_s", (P, 512), BF16, stack=p2m)
        bg_s = sb("bg_s", (P, 512), BF16, stack=p2m)
        sU_s = sb("sU_s", (P, 512), BF16, stack=p2m)
        t1_s = sb("t1_s", (P, 512), BF16, stack=p2m)
        ch_s = sb("ch_s", (P, 512), BF16, stack=p2m)
        wup_s = sb("wup_s", (P, NDC * P), BF16, stack=p2m)
        wg_s = sb("wg_s", (P, NDC * P), BF16, stack=p2m)
        upb_s = sb("upb_s", (R, NE * P), BF16, stack=p2m)
        gb_s = sb("gb_s", (R, NE * P), BF16, stack=p2m)
        dap_s = sb("dap_s", (P, 256), BF16, stack=p2m)
        wdj_s = sb("wdj_s", (P, NHC * P), BF16, stack=p2m)
        ps_b1 = psum("ps_b1", (P, 512), stack=p2m)
        ps_b2 = psum("ps_b2", (P, 512), stack=p2m)
        ps_U = psum("ps_U", (P, 512), stack=p2m)
        ps_G = psum("ps_G", (P, 512), stack=p2m)
        ps_M = psum("ps_M", (P, 512), stack=p2m)
        ps_YA = psum("ps_YA", (P, 512), stack=p2m)
        ps_YB = psum("ps_YB", (P, 512), stack=p2m)
        ps_CB = psum("ps_CB", (P, 512), stack=p2m)

        S.dma("gpsimd", upa_s[:, :], upA)
        S.dma("gpsimd", ga_s[:, :], gA)
        S.dma("gpsimd", dbTA_s[:, :].rearrange("p (c n) -> p c n", n=P),
              dBTA.rearrange("(c p) n -> p c n", p=P))
        S.dma("gpsimd", dbTB_s[:, :].rearrange("p (c n) -> p c n", n=P),
              dBTB.rearrange("(c p) n -> p c n", p=P))
        wt_ag = S.now("WT")

        xa_se = [0]
        cb_ve = [0]
        base_se = [0, 0]
        eU_se = [0]
        eG_ve = [0]
        ch_ve = [0]
        mix_se = [0]
        ya_ve = [0]
        xadd_ve = [0, 0]
        slab_te = {"upgate": 0, "wdj": 0}
        slab_wt = {"upgate": 0, "wdj": 0}

        def load_upgate(hc):
            S.dma("gpsimd", wup_s[:, :].rearrange("p (c n) -> p c n", n=P),
                  wup[hc * DM:(hc + 1) * DM, :].rearrange("(c p) n -> p c n", p=P),
                  wait=[("TE", slab_te["upgate"])])
            S.dma("gpsimd", wg_s[:, :].rearrange("p (c n) -> p c n", n=P),
                  wg[hc * DM:(hc + 1) * DM, :].rearrange("(c p) n -> p c n", p=P))
            S.dma("gpsimd", upb_s[:, :], upB[hc * R:(hc + 1) * R, :])
            S.dma("gpsimd", gb_s[:, :], gB[hc * R:(hc + 1) * R, :])
            S.dma("gpsimd", dap_s[:, :], dAp[hc * P:(hc + 1) * P, :])
            slab_wt["upgate"] = S.now("WT")

        def load_wdj(j):
            S.dma("gpsimd", wdj_s[:, :].rearrange("p (c n) -> p c n", n=P),
                  wd[j * HF:(j + 1) * HF, :].rearrange("(c p) n -> p c n", p=P),
                  wait=[("TE", slab_te["wdj"])])
            slab_wt["wdj"] = S.now("WT")

        for tt in range(2):
            for (dst, asrc) in ((xau, upa_s), (xag, ga_s)):
                for e8 in range(NE):
                    for c in range(NDC):
                        S.emit("tensor", lambda e, asrc=asrc, e8=e8, c=c, tt=tt:
                               e.matmul(ps_CB[0:R, :],
                                        asrc[:, (e8 * NDC + c) * R:(e8 * NDC + c) * R + R],
                                        XBF[:, c * TOK + tt * 512:c * TOK + tt * 512 + 512],
                                        start=(c == 0), stop=(c == NDC - 1)),
                               wait=([("WT", wt_ag), ("SE", xa_se[0]),
                                      ("VE", cb_ve[0])] if c == 0 else []),
                               inc=[("TE", 1)])
                    te_xa = S.now("TE")
                    S.emit("scalar", lambda e, dst=dst, e8=e8:
                           e.activation(out=dst[:, e8 * 512:(e8 + 1) * 512],
                                        in_=ps_CB[0:R, :], func=AF.Identity,
                                        scale=2.0),
                           wait=[("TE", te_xa)], inc=[("SE", 1)])
                    xa_se[0] = S.now("SE")
            for e8 in range(NE):
                S.dma("sync", ce_row[0:1, e8 * 512:(e8 + 1) * 512],
                      b_ce[e8:e8 + 1, tt * 512:tt * 512 + 512],
                      wait=([("LD", ld_bce), ("TE", S.now("TE"))] if e8 == 0 else []))
            ld_cer = S.now("LD")
            for e8 in range(NE):
                S.emit("tensor", lambda e, e8=e8:
                       e.matmul(ps_CB[:, :], onesb_s[0:1, :],
                                ce_row[0:1, e8 * 512:(e8 + 1) * 512],
                                start=True, stop=True),
                       wait=[("LD", ld_cer), ("SE", xa_se[0]), ("VE", cb_ve[0])],
                       inc=[("TE", 1)])
                te_cb = S.now("TE")
                S.emit("vector", lambda e, e8=e8:
                       e.tensor_copy(c_bcs[:, e8 * 512:(e8 + 1) * 512], ps_CB[:, :]),
                       wait=[("TE", te_cb)], inc=[("VE", 1)])
                cb_ve[0] = S.now("VE")
            load_upgate(0)
            for half in range(2):
                for hh in range(HHC):
                    hc = half * HHC + hh
                    wtw = slab_wt["upgate"]
                    for (wsl, pb, dst, se_i) in ((wup_s, ps_b1, bu_s, 0),
                                                 (wg_s, ps_b2, bg_s, 1)):
                        for c in range(NDC):
                            S.emit("tensor", lambda e, wsl=wsl, pb=pb, c=c, tt=tt:
                                   e.matmul(pb[:, :], wsl[:, c * P:(c + 1) * P],
                                            XBF[:, c * TOK + tt * 512:
                                                c * TOK + tt * 512 + 512],
                                            start=(c == 0), stop=(c == NDC - 1)),
                                   wait=([("WT", wtw), ("SE", base_se[se_i]),
                                          ("VE", ch_ve[0]), ("VE", xadd_ve[0]),
                                          ("VE", xadd_ve[1])] if c == 0 else []),
                                   inc=[("TE", 1)])
                        te_b = S.now("TE")
                        S.emit("scalar", lambda e, pb=pb, dst=dst:
                               e.activation(out=dst[:, :], in_=pb[:, :], func=AF.Copy),
                               wait=[("TE", te_b)], inc=[("SE", 1)])
                        base_se[se_i] = S.now("SE")
                    for e8 in range(NE):
                        S.emit("tensor", lambda e, e8=e8:
                               e.matmul(ps_U[:, :], upb_s[0:R, e8 * P:(e8 + 1) * P],
                                        xau[0:R, e8 * 512:(e8 + 1) * 512],
                                        start=True, stop=False),
                               wait=[("SE", base_se[0]), ("SE", eU_se[0]),
                                     ("VE", eG_ve[0])],
                               inc=[("TE", 1)])
                        S.emit("tensor", lambda e:
                               e.matmul(ps_U[:, :], idb_s[:, :], bu_s[:, :],
                                        start=False, stop=True),
                               inc=[("TE", 1)])
                        S.emit("tensor", lambda e, e8=e8:
                               e.matmul(ps_G[:, :], gb_s[0:R, e8 * P:(e8 + 1) * P],
                                        xag[0:R, e8 * 512:(e8 + 1) * 512],
                                        start=True, stop=False),
                               wait=[("SE", base_se[1])],
                               inc=[("TE", 1)])
                        S.emit("tensor", lambda e:
                               e.matmul(ps_G[:, :], idb_s[:, :], bg_s[:, :],
                                        start=False, stop=True),
                               inc=[("TE", 1)])
                        te_UG = S.now("TE")
                        S.emit("scalar", lambda e:
                               e.activation(out=sU_s[:, :], in_=ps_U[:, :],
                                            func=AF.Silu),
                               wait=[("TE", te_UG)], inc=[("SE", 1)])
                        eU_se[0] = S.now("SE")
                        S.emit("vector", lambda e, e8=e8:
                               e.tensor_mul(t1_s[:, :], ps_G[:, :],
                                            c_bcs[:, e8 * 512:(e8 + 1) * 512]),
                               wait=[("TE", te_UG)], inc=[("VE", 1)])
                        eG_ve[0] = S.now("VE")
                        S.emit("vector", lambda e:
                               e.tensor_mul(ch_s[:, :], t1_s[:, :], sU_s[:, :]),
                               wait=[("SE", eU_se[0])], inc=[("VE", 1)])
                        ch_ve[0] = S.now("VE")
                        S.emit("tensor", lambda e, e8=e8:
                               e.matmul(ps_M[:, :], idb_s[:, :], ch_s[:, :],
                                        start=(e8 == 0), stop=(e8 == NE - 1)),
                               wait=[("VE", ch_ve[0]), ("SE", mix_se[0])],
                               inc=[("TE", 1)])
                        S.emit("tensor", lambda e, e8=e8, hc=hc:
                               e.matmul((ps_YA if e8 < 4 else ps_YB)
                                        [32 * (e8 % 4):32 * (e8 % 4) + 32, :],
                                        dap_s[:, 32 * e8:32 * e8 + 32],
                                        ch_s[:, :],
                                        start=(hc == 0), stop=(hc == NHC - 1),
                                        tile_position=(0, 32 * (e8 % 4)),
                                        skip_group_check=True),
                               wait=([("VE", ya_ve[0])] if hc == 0 and e8 == 0 else []),
                               inc=[("TE", 1)])
                    te_mix = S.now("TE")
                    S.emit("scalar", lambda e, hh=hh:
                           e.activation(out=mixed[:, hh * 512:(hh + 1) * 512],
                                        in_=ps_M[:, :], func=AF.Copy),
                           wait=[("TE", te_mix)], inc=[("SE", 1)])
                    mix_se[0] = S.now("SE")
                    slab_te["upgate"] = S.now("TE")
                    if hc + 1 < NHC:
                        load_upgate(hc + 1)
                    elif tt == 0:
                        load_upgate(0)
                load_wdj(0)
                for j in range(NDC):
                    if half == 1 and j == 0:
                        ya_prev_te = S.now("TE")
                        S.emit("vector", lambda e:
                               e.tensor_scalar(ya_s[:, 0:512], ps_YA[:, :],
                                               2.0, None, op0=ALU.mult),
                               wait=[("TE", ya_prev_te)], inc=[("VE", 1)])
                        S.emit("vector", lambda e:
                               e.tensor_scalar(ya_s[:, 512:1024], ps_YB[:, :],
                                               2.0, None, op0=ALU.mult),
                               inc=[("VE", 1)])
                        ya_ve[0] = S.now("VE")
                    pbj = ps_M if j % 2 == 0 else ps_b1
                    for hh in range(HHC):
                        S.emit("tensor", lambda e, hh=hh, j=j, pbj=pbj, half=half:
                               e.matmul(pbj[:, :],
                                        wdj_s[:, (half * HHC + hh) * P:
                                              (half * HHC + hh) * P + P],
                                        mixed[:, hh * 512:(hh + 1) * 512],
                                        start=(hh == 0),
                                        stop=(half == 0 and hh == HHC - 1)),
                               wait=([("WT", slab_wt["wdj"]), ("SE", mix_se[0]),
                                      ("VE", xadd_ve[j % 2]), ("SE", base_se[1])]
                                     if hh == 0 else []),
                               inc=[("TE", 1)])
                    if half == 1:
                        S.emit("tensor", lambda e, j=j, pbj=pbj:
                               e.matmul(pbj[:, :], dbTA_s[:, j * P:(j + 1) * P],
                                        ya_s[:, 0:512], start=False, stop=False),
                               wait=[("VE", ya_ve[0])],
                               inc=[("TE", 1)])
                        S.emit("tensor", lambda e, j=j, pbj=pbj:
                               e.matmul(pbj[:, :], dbTB_s[:, j * P:(j + 1) * P],
                                        ya_s[:, 512:1024], start=False, stop=True),
                               inc=[("TE", 1)])
                    te_out = S.now("TE")
                    S.emit("vector", lambda e, j=j, tt=tt, pbj=pbj:
                           e.tensor_add(X32[:, j * TOK + tt * 512:
                                            j * TOK + tt * 512 + 512],
                                        X32[:, j * TOK + tt * 512:
                                            j * TOK + tt * 512 + 512],
                                        pbj[:, :]),
                           wait=[("TE", te_out)], inc=[("VE", 1)])
                    xadd_ve[j % 2] = S.now("VE")
                    slab_te["wdj"] = S.now("TE")
                    if j + 1 < NDC:
                        load_wdj(j + 1)
                    elif not (tt == 1 and half == 1):
                        load_wdj(0)
        S.flush()
        p2m.close()
        if not DBG_SKIP_LN2:
            p2ln = ExitStack()
            ln_arena2 = sb("ln_arena2", (P, NDC * TOK), BF16, stack=p2ln)
            layer_norm(ln_arena2)
            S.barrier()
            S.flush()
            p2ln.close()



    if (not DBG_SKIP_LN) and n_phases >= 3:
        S.barrier()
        p3 = ExitStack()
        vfm = sb("vfm", (P, NH * TOK), BF16, stack=p3)
        kh_s = [sb(f"kh{i}", (P, SEQ), BF16, stack=p3) for i in range(2)]
        probs = sb("probs", (P, SEQ), BF16, stack=p3)
        qh_s = [sb(f"qh{i}", (P, TOK), BF16, stack=p3) for i in range(2)]
        qn2 = sb("qn2", (1, NH), F32, stack=p3)
        kn2 = sb("kn2", (1, NH), F32, stack=p3)
        kng = sb("kng", (1, NH), F32, stack=p3)
        nbrow = sb("nbrow", (1, NH), F32, stack=p3)
        negb_bc = sb("negb_bc", (P, NH), F32, stack=p3)
        zpart = sb("zpart", (P, 8), F32, stack=p3)
        zrow = sb("zrow", (P, 2), F32, stack=p3)
        bb32 = sb("bb32", (P, 32), BF16, stack=p3)

        wo_s = sb("wo3", (P, NDC * P), BF16, stack=p3)
        p3s = ExitStack()
        qkv_s = [sb(f"qkv{i}", (P, NDC * P), BF16, stack=p3s) for i in range(2)]
        qstg = [sb(f"qstg{i}", (P, TOK), BF16, stack=p3s) for i in range(2)]
        sqscr3 = sb("sqscr3", (P, TOK), BF16, stack=p3s)
        ps_a = psum("ps_a3", (P, TOK), stack=p3)
        ps_b = psum("ps_b3", (P, TOK), stack=p3)
        p3a = ExitStack()
        ps_nrm = psum("ps_nrm", (1, TOK), stack=p3a)

        qkv_wt = [0, 0]
        qkv_te = [0, 0]
        stg_ld = [0, 0]
        stg_se = [0, 0]
        nrm_te = [0]
        nrm_ve = [0]

        def load_qkv(f):
            b = f % 2
            S.dma("gpsimd", qkv_s[b][:, :].rearrange("p (c n) -> p c n", n=P),
                  w1qkv[f * DM:(f + 1) * DM, :].rearrange("(c p) n -> p c n", p=P),
                  wait=[("TE", qkv_te[b])])
            qkv_wt[b] = S.now("WT")

        load_qkv(0)
        load_qkv(1)
        ve_x2 = S.now("VE")
        for f in range(48):
            b = f % 2
            for tt in range(2):
                pb = ps_a if tt == 0 else ps_b
                for c in range(NDC):
                    S.emit("tensor", lambda e, b=b, c=c, tt=tt, pb=pb:
                           e.matmul(pb[:, 0:512], qkv_s[b][:, c * P:(c + 1) * P],
                                    XBF[:, c * TOK + tt * 512:c * TOK + tt * 512 + 512],
                                    start=(c == 0), stop=(c == NDC - 1)),
                           wait=([("WT", qkv_wt[b]), ("SE", stg_se[tt]),
                                  ("VE", ve_x2)] if c == 0 else []),
                           inc=[("TE", 1)])
            te_f = S.now("TE")
            qkv_te[b] = te_f
            if f + 2 < 48:
                load_qkv(f + 2)
            if f < 16:
                h = f
                sb_ = qstg[f % 2]
                for tt in range(2):
                    S.emit("scalar", lambda e, sb_=sb_, tt=tt:
                           e.activation(out=sb_[:, tt * 512:tt * 512 + 512],
                                        in_=(ps_a if tt == 0 else ps_b)[:, 0:512],
                                        func=AF.Copy),
                           wait=[("TE", te_f), ("LD", stg_ld[f % 2])],
                           inc=[("SE", 1)])
                    stg_se[tt] = S.now("SE")
                se_q = S.now("SE")
                S.emit("scalar", lambda e, sb_=sb_:
                       e.activation(out=sqscr3[:, :], in_=sb_[:, :],
                                    func=AF.Square),
                       wait=[("SE", se_q), ("TE", nrm_te[0])], inc=[("SE", 1)])
                se_sq3 = S.now("SE")
                for tt in range(2):
                    S.emit("tensor", lambda e, tt=tt:
                           e.matmul(ps_nrm[0:1, tt * 512:tt * 512 + 512],
                                    onesb_s[:, 0:1],
                                    sqscr3[:, tt * 512:tt * 512 + 512],
                                    start=True, stop=True),
                           wait=([("SE", se_sq3), ("VE", nrm_ve[0])]
                                 if tt == 0 else []),
                           inc=[("TE", 1)])
                nrm_te[0] = S.now("TE")
                S.emit("vector", lambda e, h=h:
                       e.reduce_max(qn2[0:1, h:h + 1], ps_nrm[0:1, :], axis=AX.X),
                       wait=[("TE", nrm_te[0])], inc=[("VE", 1)], vspecial=True)
                nrm_ve[0] = S.now("VE")
                S.dma("sync", qd[h * P:(h + 1) * P, :], sb_[:, :],
                      wait=[("SE", se_q)])
                stg_ld[f % 2] = S.now("LD")
            elif f < 32:
                h = f - 16
                sb_ = qstg[f % 2]
                for tt in range(2):
                    S.emit("scalar", lambda e, sb_=sb_, tt=tt:
                           e.activation(out=sb_[:, tt * 512:tt * 512 + 512],
                                        in_=(ps_a if tt == 0 else ps_b)[:, 0:512],
                                        func=AF.Copy),
                           wait=[("TE", te_f), ("LD", stg_ld[f % 2])],
                           inc=[("SE", 1)])
                    stg_se[tt] = S.now("SE")
                se_k = S.now("SE")
                S.emit("scalar", lambda e, sb_=sb_:
                       e.activation(out=sqscr3[:, :], in_=sb_[:, :],
                                    func=AF.Square),
                       wait=[("SE", se_k), ("TE", nrm_te[0])], inc=[("SE", 1)])
                se_sq3 = S.now("SE")
                for tt in range(2):
                    S.emit("tensor", lambda e, tt=tt:
                           e.matmul(ps_nrm[0:1, tt * 512:tt * 512 + 512],
                                    onesb_s[:, 0:1],
                                    sqscr3[:, tt * 512:tt * 512 + 512],
                                    start=True, stop=True),
                           wait=([("SE", se_sq3), ("VE", nrm_ve[0])]
                                 if tt == 0 else []),
                           inc=[("TE", 1)])
                nrm_te[0] = S.now("TE")
                S.emit("vector", lambda e, h=h:
                       e.reduce_max(kn2[0:1, h:h + 1], ps_nrm[0:1, :], axis=AX.X),
                       wait=[("TE", nrm_te[0])], inc=[("VE", 1)], vspecial=True)
                nrm_ve[0] = S.now("VE")
                S.dma("sync", k_in[h * P:(h + 1) * P, :], sb_[:, :],
                      wait=[("SE", se_k)])
                stg_ld[f % 2] = S.now("LD")
                if f == 31:
                    ld_kall = S.now("LD")
            else:
                h = f - 32
                for tt in range(2):
                    S.emit("scalar", lambda e, h=h, tt=tt:
                           e.activation(out=vfm[:, h * TOK + tt * 512:
                                                 h * TOK + tt * 512 + 512],
                                        in_=(ps_a if tt == 0 else ps_b)[:, 0:512],
                                        func=AF.Copy),
                           wait=[("TE", te_f)], inc=[("SE", 1)])
                    stg_se[tt] = S.now("SE")
        se_qkv = S.now("SE")
        cc_k = S.collective(lambda e: e.collective_compute(
            "AllGather", ALU.bypass,
            replica_groups=[list(range(NCORES))],
            ins=[k_in.ap().opt()], outs=[k_out.ap().opt()]),
            wait=[("LD", ld_kall)])

        S.vserial = True
        ve_nrm = S.now("VE")
        S.dma("sync", km_in.ap(), kn2[0:1, :],
              wait=[("VE", ve_nrm), ("CC", cc_k)])
        ld_km = S.now("LD")
        cc_km = S.collective(lambda e: e.collective_compute(
            "AllReduce", ALU.max, replica_groups=[list(range(NCORES))],
            ins=[km_in.ap().opt()], outs=[km_out.ap().opt()]),
            wait=[("LD", ld_km)])
        S.dma("sync", kng[0:1, :], km_out.ap(), wait=[("CC", cc_km)])
        ld_kmg = S.now("LD")
        S.emit("vector", lambda e: e.tensor_mul(nbrow[0:1, :], qn2[0:1, :],
                                                kng[0:1, :]),
               wait=[("LD", ld_kmg)], inc=[("VE", 1)])
        se_nbs = S.now("SE")
        S.emit("scalar", lambda e: e.activation(out=nbrow[0:1, :],
                                                in_=nbrow[0:1, :], func=AF.Sqrt),
               wait=[("VE", S.now("VE"))], inc=[("SE", 1)])
        se_nb = S.now("SE")
        S.emit("vector", lambda e: e.tensor_scalar(nbrow[0:1, :], nbrow[0:1, :],
                                                   -(float(DK) ** -0.5), None,
                                                   op0=ALU.mult),
               wait=[("SE", se_nb)], inc=[("VE", 1)])
        ld_nb = S.now("LD")
        for h in range(NH):
            S.emit("tensor", lambda e, h=h:
                   e.matmul(ps_a[:, 0:1], onesf_s[0:1, :], nbrow[0:1, h:h + 1],
                            start=True, stop=True),
                   wait=([("LD", ld_nb), ("VE", S.now("VE")),
                          ("SE", se_qkv)] if h == 0 else
                         [("LD", ld_nb), ("VE", S.now("VE"))]),
                   inc=[("TE", 1)])
            te_nb = S.now("TE")
            S.emit("vector", lambda e, h=h:
                   e.tensor_copy(negb_bc[:, h:h + 1], ps_a[:, 0:1]),
                   wait=[("TE", te_nb)], inc=[("VE", 1)])
        S.vserial = False
        ve_nbc = S.now("VE")
        S.flush()
        p3a.close()
        p3s.close()
        cs_sb = sb("cs_sb", (P, 4 * 512), F32, stack=p3)
        crow = sb("crow", (1, TOK), BF16, stack=p3)
        ps_cs = [psum(f"ps_cs{i}", (P, 512), stack=p3) for i in range(4)]

        kh_wt = [0, 0]
        kh_te = [0, 0]
        qh_wt = [0, 0]

        def load_head(h):
            b = h % 2
            for r in range(NCORES):
                S.dma("sync", kh_s[b][:, r * TOK:(r + 1) * TOK],
                      k_out[r * DM + h * P: r * DM + (h + 1) * P, :],
                      wait=([("CC", cc_km), ("TE", kh_te[b])] if r == 0 else []))
            S.dma("sync", qh_s[b][:, :], qd[h * P:(h + 1) * P, :])
            kh_wt[b] = S.now("LD")

        load_head(0)
        load_head(1)
        if DBG_SKIP_LN3:
            dbg3v1 = nc.dram_tensor("dbg3v1", [P, NH * TOK], BF16,
                                    kind="ExternalOutput").ap()
            S.dma("sync", dbg3v1, vfm[:, :], wait=[("SE", se_qkv)], sem="ST")
        exp_se = [0]
        cs_ve = [0]
        cs_te = [0]
        z_ve = [0]
        zred_ve = [0]
        cs_ld_prev = [0]
        for h in range(NH):
            b = h % 2
            S.emit("vector", lambda e: e.memset(cs_sb[:, :], 0.0),
                   wait=[("TE", S.now("TE")), ("LD", cs_ld_prev[0])],
                   inc=[("VE", 1)])
            csacc_ve = [S.now("VE")]
            for qt in range(8):
                for grp in range(8):
                    pg = ps_a if grp % 2 == 0 else ps_b
                    for half2 in range(2):
                        S.emit("tensor", lambda e, b=b, qt=qt, grp=grp,
                               half2=half2, pg=pg:
                               e.matmul(pg[:, half2 * 512:half2 * 512 + 512],
                                        qh_s[b][:, qt * P:(qt + 1) * P],
                                        kh_s[b][:, grp * TOK + half2 * 512:
                                                grp * TOK + half2 * 512 + 512],
                                        start=True, stop=True),
                               wait=([("LD", kh_wt[b]), ("SE", exp_se[0]),
                                      ("VE", ve_nbc)]
                                     if (qt == 0 and grp < 2 and half2 == 0)
                                     else ([("SE", exp_se[0])]
                                           if half2 == 0 else [])),
                               inc=[("TE", 1)])
                    te_sc = S.now("TE")
                    S.emit("scalar", lambda e, h=h, qt=qt, grp=grp, pg=pg:
                           e.activation(out=probs[:, grp * TOK:(grp + 1) * TOK],
                                        in_=pg[:, :], func=AF.Exp,
                                        scale=ATT_SCALE,
                                        bias=negb_bc[:, h:h + 1],
                                        accum_out=zpart[:, grp:grp + 1]),
                           wait=[("TE", te_sc), ("TE", cs_te[0]),
                                 ("VE", zred_ve[0])],
                           inc=[("SE", 1)])
                    exp_se[0] = S.now("SE")
                S.emit("vector", lambda e:
                       e.reduce_sum(zrow[:, 0:1], zpart[:, :], axis=AX.X),
                       wait=[("SE", exp_se[0])], inc=[("VE", 1)], vspecial=True)
                zred_ve[0] = S.now("VE")
                S.emit("vector", lambda e:
                       e.reciprocal(zrow[:, 1:2], zrow[:, 0:1]),
                       inc=[("VE", 1)], vspecial=True)
                S.emit("vector", lambda e:
                       e.tensor_scalar(bb32[:, :], onesb_s[:, 0:32],
                                       zrow[:, 1:2], None, op0=ALU.mult),
                       wait=[("TE", cs_te[0])], inc=[("VE", 1)])
                z_ve[0] = S.now("VE")
                for s in range(16):
                    S.emit("tensor", lambda e, s=s:
                           e.matmul(ps_cs[s // 4][32 * (s % 4):32 * (s % 4) + 32, :],
                                    bb32[:, :], probs[:, s * 512:(s + 1) * 512],
                                    start=True, stop=True,
                                    tile_position=(0, 32 * (s % 4)),
                                    skip_group_check=True),
                           wait=([("VE", z_ve[0]), ("VE", csacc_ve[0])]
                                 if s == 0 else []),
                           inc=[("TE", 1)])
                cs_te[0] = S.now("TE")
                for bk in range(4):
                    S.emit("vector", lambda e, bk=bk:
                           e.tensor_add(cs_sb[:, bk * 512:(bk + 1) * 512],
                                        cs_sb[:, bk * 512:(bk + 1) * 512],
                                        ps_cs[bk][:, :]),
                           wait=([("TE", cs_te[0])] if bk == 0 else []),
                           inc=[("VE", 1)])
                csacc_ve[0] = S.now("VE")
            ve_csf = csacc_ve[0]
            for s in range(16):
                S.dma("sync", cs_in[(s // 2) * NH + h:(s // 2) * NH + h + 1,
                                    (s % 2) * 512:(s % 2) * 512 + 512],
                      cs_sb[32 * (s % 4):32 * (s % 4) + 1,
                            (s // 4) * 512:(s // 4) * 512 + 512],
                      wait=([("VE", ve_csf)] if s == 0 else []))
            cs_ld = S.now("LD")
            cs_ld_prev[0] = cs_ld
            kh_te[b] = S.now("TE")
            if h + 2 < NH:
                load_head(h + 2)
            if h == NH - 1:
                ld_csall = S.now("LD")
        if DBG_SKIP_LN3:
            dbg_ci = nc.dram_tensor("dbg_ci", [NCORES * NH, TOK], F32,
                                    kind="ExternalOutput").ap()
            S.dma("sync", dbg_ci, cs_in.ap(), wait=[("LD", ld_csall)], sem="ST")
            dbg_cr = nc.dram_tensor("dbg_cr", [NH, TOK], F32,
                                    kind="ExternalOutput").ap()
        cc_cs = S.collective(lambda e: e.collective_compute(
            "ReduceScatter", ALU.add, replica_groups=[list(range(NCORES))],
            ins=[cs_in.ap().opt()], outs=[cs_rs.ap().opt()]),
            wait=[("LD", ld_csall)])
        if DBG_SKIP_LN3:
            S.dma("sync", dbg_cr, cs_rs.ap(), wait=[("CC", cc_cs)], sem="ST")

        crow_ld = [0]
        val_ve = [0]
        pa_st = [0]
        if DBG_SKIP_LN3:
            dbg3v0 = nc.dram_tensor("dbg3v0", [P, NH * TOK], BF16,
                                    kind="ExternalOutput").ap()
            dbg_pa = nc.dram_tensor("dbg_pa", [P, NH * TOK], F32,
                                    kind="ExternalOutput").ap()
            pa_sb = sb("pa_sb", (P, TOK), F32, stack=p3)
            S.dma("sync", dbg3v0, vfm[:, :], wait=[("SE", se_qkv)], sem="ST")
            pa_st[0] = S.now("ST")
        for h in range(NH):
            S.dma("gpsimd", crow[0:1, :], cs_rs[h:h + 1, :],
                  wait=[("CC", cc_cs), ("VE", val_ve[0])], sem="LD")
            crow_ld[0] = S.now("LD")
            for tt in range(2):
                S.emit("tensor", lambda e, tt=tt:
                       e.matmul(ps_a[:, tt * 512:tt * 512 + 512], onesb_s[0:1, :],
                                crow[0:1, tt * 512:tt * 512 + 512],
                                start=True, stop=True),
                       wait=[("LD", crow_ld[0]), ("VE", val_ve[0]),
                             ("ST", pa_st[0])],
                       inc=[("TE", 1)])
            te_cb = S.now("TE")
            if DBG_SKIP_LN3:
                S.emit("scalar", lambda e:
                       e.activation(out=pa_sb[:, :], in_=ps_a[:, :],
                                    func=AF.Copy),
                       wait=[("TE", te_cb), ("ST", pa_st[0])], inc=[("SE", 1)])
                se_pa = S.now("SE")
                S.dma("sync", dbg_pa[:, h * TOK:(h + 1) * TOK], pa_sb[:, :],
                      wait=[("SE", se_pa)], sem="ST")
                pa_st[0] = S.now("ST")
            for tt in range(2):
                S.emit("vector", lambda e, h=h, tt=tt:
                       e.tensor_mul(vfm[:, h * TOK + tt * 512:
                                        h * TOK + tt * 512 + 512],
                                    vfm[:, h * TOK + tt * 512:
                                        h * TOK + tt * 512 + 512],
                                    ps_a[:, tt * 512:tt * 512 + 512]),
                       wait=([("TE", te_cb), ("ST", pa_st[0])]
                             if tt == 0 else []),
                       inc=[("VE", 1)])
            val_ve[0] = S.now("VE")
        ve_val = S.now("VE")

        wo_wt3 = [0]
        wo_te3 = [0]
        wo_ve3 = [0, 0]

        def load_wo3(j):
            S.dma("gpsimd", wo_s[:, :].rearrange("p (c n) -> p c n", n=P),
                  w1o[j * DM:(j + 1) * DM, :].rearrange("(c p) n -> p c n", p=P),
                  wait=[("TE", wo_te3[0])])
            wo_wt3[0] = S.now("WT")

        load_wo3(0)
        for j in range(NDC):
            for tt in range(2):
                pb = ps_a if tt == 0 else ps_b
                for c in range(NDC):
                    S.emit("tensor", lambda e, c=c, tt=tt, pb=pb:
                           e.matmul(pb[:, 0:512], wo_s[:, c * P:(c + 1) * P],
                                    vfm[:, c * TOK + tt * 512:
                                        c * TOK + tt * 512 + 512],
                                    start=(c == 0), stop=(c == NDC - 1)),
                           wait=([("WT", wo_wt3[0]), ("VE", ve_val),
                                  ("VE", wo_ve3[tt])] if c == 0 else []),
                           inc=[("TE", 1)])
            te_wo = S.now("TE")
            wo_te3[0] = te_wo
            if j + 1 < NDC:
                load_wo3(j + 1)
            for tt in range(2):
                pb = ps_a if tt == 0 else ps_b
                S.emit("vector", lambda e, j=j, tt=tt, pb=pb:
                       e.tensor_add(X32[:, j * TOK + tt * 512:
                                        j * TOK + tt * 512 + 512],
                                    X32[:, j * TOK + tt * 512:
                                        j * TOK + tt * 512 + 512],
                                    pb[:, 0:512]),
                       wait=[("TE", te_wo)], inc=[("VE", 1)])
                wo_ve3[tt] = S.now("VE")
        if DBG_SKIP_LN3:
            dbg3 = nc.dram_tensor("dbg3", [P, NH + 16 + NH * 8], F32,
                                  kind="ExternalOutput").ap()
            dbg3v = nc.dram_tensor("dbg3v", [P, NH * TOK], BF16,
                                   kind="ExternalOutput").ap()
            dbg_xbf = nc.dram_tensor("dbg_xbf", [P, NDC * TOK], BF16,
                                     kind="ExternalOutput").ap()
            S.barrier()
            S.dma("sync", dbg3[:, 0:NH], negb_bc[:, :], sem="ST")
            S.dma("sync", dbg3[:, NH:NH + 8], zpart[:, :], sem="ST")
            S.dma("sync", dbg3[:, NH + 8:NH + 10], zrow[:, :], sem="ST")
            S.dma("sync", dbg3v, vfm[:, :], sem="ST")
            S.dma("sync", dbg_xbf, XBF[:, :], sem="ST")
        S.flush()
        p3.close()
        if not DBG_SKIP_LN3:
            p3ln = ExitStack()
            ln_arena3 = sb("ln_arena3", (P, NDC * TOK), BF16, stack=p3ln)
            layer_norm(ln_arena3)
            S.barrier()
            S.flush()
            p3ln.close()

    if n_phases >= 4 and not DBG_SKIP_LN3:
        p4 = ExitStack()
        xab = sb("xab", (P, DM), F32, stack=p4)
        absr = sb("absr", (P, 8), F32, stack=p4)
        q4s = sb("q4s", (P, 32), F32, stack=p4)
        rrow = sb("rrow", (1, TOK), F32, stack=p4)
        irow = sb("irow", (1, TOK), F32, stack=p4)
        qtmp = sb("qtmp", (P, 512), F32, stack=p4)
        w4_s = [sb(f"w4_{i}", (P, NDC * P), BF16, stack=p4) for i in range(2)]
        w4c_s = sb("w4c", (P, NDC), F32, stack=p4)
        w4b_s = sb("w4b", (P, NDC), F32, stack=p4)
        etmp = [sb(f"etmp{i}", (P, 512), F32, stack=p4) for i in range(2)]
        f16t = [sb(f"f16t{i}", (P, 512), F16, stack=p4) for i in range(2)]
        isc_sb = sb("isc_sb", (P, TOK), F32, stack=p4)
        ps4a = psum("ps4a", (P, TOK), stack=p4)
        ps4b = psum("ps4b", (P, TOK), stack=p4)
        ps4m = [psum(f"ps4m{i}", (P, 512), stack=p4) for i in range(2)]

        cc_all = S.now("CC")
        S.dma("sync", w4c_s[:, :], w1sc, wait=[("CC", cc_all)])
        S.dma("sync", w4b_s[:, :], w1b)
        ld_wsb = S.now("LD")

        trans_se = [0, 0]
        red_ve = [0]
        for g in range(8):
            for c in range(NDC):
                b = (g * NDC + c) % 2
                S.emit("tensor", lambda e, g=g, c=c, b=b:
                       e.transpose(ps4m[b][:, 0:P],
                                   X32[:, c * TOK + g * P:c * TOK + (g + 1) * P],
                                   idf_s[:, :]),
                       wait=[("SE", trans_se[b])], inc=[("TE", 1)])
                te_t = S.now("TE")
                S.emit("scalar", lambda e, c=c, b=b:
                       e.activation(out=xab[:, c * P:(c + 1) * P],
                                    in_=ps4m[b][:, 0:P], func=AF.Abs),
                       wait=[("TE", te_t), ("VE", red_ve[0])], inc=[("SE", 1)])
                trans_se[b] = S.now("SE")
            S.emit("vector", lambda e, g=g:
                   e.reduce_max(absr[:, g:g + 1], xab[:, :], axis=AX.X),
                   wait=[("SE", S.now("SE"))], inc=[("VE", 1)], vspecial=True)
            red_ve[0] = S.now("VE")

        S.vserial = True
        S.emit("vector", lambda e: e.tensor_scalar(q4s[:, 0:8], absr[:, :],
                                                   const_s[:, 1:2], None,
                                                   op0=ALU.max),
               wait=[("VE", red_ve[0])], inc=[("VE", 1)])
        S.emit("vector", lambda e: e.reciprocal(q4s[:, 8:16], q4s[:, 0:8]),
               inc=[("VE", 1)], vspecial=True)
        S.emit("vector", lambda e: e.tensor_mul(q4s[:, 16:24], q4s[:, 0:8],
                                                q4s[:, 8:16]), inc=[("VE", 1)])
        S.emit("vector", lambda e: e.tensor_scalar(q4s[:, 16:24], q4s[:, 16:24],
                                                   -1.0, 2.0, op0=ALU.mult,
                                                   op1=ALU.add), inc=[("VE", 1)])
        S.emit("vector", lambda e: e.tensor_mul(q4s[:, 8:16], q4s[:, 8:16],
                                                q4s[:, 16:24]), inc=[("VE", 1)])
        S.emit("vector", lambda e: e.tensor_scalar(q4s[:, 8:16], q4s[:, 8:16],
                                                   127.0, None, op0=ALU.mult),
               inc=[("VE", 1)])
        S.emit("vector", lambda e: e.tensor_scalar(q4s[:, 24:32], q4s[:, 0:8],
                                                   1.0 / 127.0, None,
                                                   op0=ALU.mult),
               inc=[("VE", 1)])
        S.vserial = False
        ve_q4 = S.now("VE")

        S.dma("sync", b_rq[:, 0:8], q4s[:, 8:16], wait=[("VE", ve_q4)])
        S.dma("sync", b_rq[:, 8:16], q4s[:, 24:32])
        ld_brq = S.now("LD")
        for g in range(8):
            S.emit("sync", lambda e, g=g: e.dma_start(
                out=rrow[0:1, g * P:(g + 1) * P].rearrange(
                    "o (c s) -> o c s", s=1),
                in_=b_rq[:, g:g + 1].rearrange("(o c) s -> o c s", o=1),
                allow_slow_non_contiguous=True),
                wait=([("LD", ld_brq)] if g == 0 else []), inc=[("LD", 16)])
            S.emit("sync", lambda e, g=g: e.dma_start(
                out=irow[0:1, g * P:(g + 1) * P].rearrange(
                    "o (c s) -> o c s", s=1),
                in_=b_rq[:, 8 + g:9 + g].rearrange("(o c) s -> o c s", o=1),
                allow_slow_non_contiguous=True), inc=[("LD", 16)])
        ld_rows = S.now("LD")

        for tt in range(2):
            S.emit("tensor", lambda e, tt=tt:
                   e.matmul(ps4a[:, tt * 512:tt * 512 + 512], onesf_s[0:1, :],
                            rrow[0:1, tt * 512:tt * 512 + 512],
                            start=True, stop=True),
                   wait=([("LD", ld_rows)] if tt == 0 else []), inc=[("TE", 1)])
            S.emit("tensor", lambda e, tt=tt:
                   e.matmul(ps4b[:, tt * 512:tt * 512 + 512], onesf_s[0:1, :],
                            irow[0:1, tt * 512:tt * 512 + 512],
                            start=True, stop=True), inc=[("TE", 1)])
        te_bc4 = S.now("TE")
        S.emit("scalar", lambda e: e.activation(out=isc_sb[:, :],
                                                in_=ps4b[:, :], func=AF.Copy),
               wait=[("TE", te_bc4)], inc=[("SE", 1)])
        se_isc = S.now("SE")

        for c in range(NDC):
            for tt in range(2):
                csl = slice(c * TOK + tt * 512, c * TOK + tt * 512 + 512)
                S.emit("vector", lambda e, csl=csl, tt=tt:
                       e.tensor_mul(qtmp[:, :], X32[:, csl],
                                    ps4a[:, tt * 512:tt * 512 + 512]),
                       wait=([("TE", te_bc4)] if (c == 0 and tt == 0) else []),
                       inc=[("VE", 1)])
                S.emit("vector", lambda e, csl=csl:
                       e.tensor_scalar(XBF[:, csl], qtmp[:, :], MAGIC, -MAGIC,
                                       op0=ALU.add, op1=ALU.add),
                       inc=[("VE", 1)])
        qz_ve = S.now("VE")

        w4_wt = [0, 0]
        w4_te = [0, 0]
        epi_ve = [0, 0]
        f16_ve = [0, 0]

        def load_w4(j):
            b = j % 2
            S.dma("sync", w4_s[b][:, :].rearrange("p (c n) -> p c n", n=P),
                  w1T[j * DM:(j + 1) * DM, :].rearrange("(c p) n -> p c n", p=P),
                  wait=[("TE", w4_te[b]), ("CC", cc_all)], sem="WT")
            w4_wt[b] = S.now("WT")

        load_w4(0)
        load_w4(1)
        for j in range(NDC):
            b = j % 2
            for tt in range(2):
                for c in range(NDC):
                    S.emit("tensor", lambda e, b=b, c=c, tt=tt:
                           e.matmul(ps4m[tt][:, :],
                                    w4_s[b][:, c * P:(c + 1) * P],
                                    XBF[:, c * TOK + tt * 512:
                                        c * TOK + tt * 512 + 512],
                                    start=(c == 0), stop=(c == NDC - 1)),
                           wait=([("WT", w4_wt[b]), ("VE", qz_ve),
                                  ("VE", epi_ve[tt])] if c == 0 else []),
                           inc=[("TE", 1)])
            te_j = S.now("TE")
            w4_te[b] = te_j
            if j + 2 < NDC:
                load_w4(j + 2)
            for tt in range(2):
                S.emit("vector", lambda e, tt=tt:
                       e.tensor_mul(etmp[tt][:, :], ps4m[tt][:, :],
                                    isc_sb[:, tt * 512:tt * 512 + 512]),
                       wait=([("TE", te_j), ("SE", se_isc)] if j == 0
                             else [("TE", te_j)]), inc=[("VE", 1)])
                epi_ve[tt] = S.now("VE")
                S.emit("scalar", lambda e, j=j, tt=tt:
                       e.activation(out=f16t[tt][:, :], in_=etmp[tt][:, :],
                                    func=AF.Identity, scale=w4c_s[:, j:j + 1],
                                    bias=w4b_s[:, j:j + 1]),
                       wait=[("VE", epi_ve[tt]), ("LD", ld_wsb)],
                       inc=[("SE", 1)])
                se_f = S.now("SE")
                S.emit("vector", lambda e, j=j, tt=tt:
                       e.tensor_add(X32[:, j * TOK + tt * 512:
                                        j * TOK + tt * 512 + 512],
                                    X32[:, j * TOK + tt * 512:
                                        j * TOK + tt * 512 + 512],
                                    f16t[tt][:, :]),
                       wait=[("SE", se_f)], inc=[("VE", 1)])
                f16_ve[tt] = S.now("VE")
        S.barrier()
        S.flush()
        p4.close()
        p4ln = ExitStack()
        ln_arena4 = sb("ln_arena4", (P, NDC * TOK), BF16, stack=p4ln)
        layer_norm(ln_arena4)
        S.barrier()
        S.flush()
        p4ln.close()

    S.barrier()
    S.dma("sync", out.rearrange("(c p) n -> p c n", p=P),
          X32[:, :].rearrange("p (c n) -> p c n", n=TOK), sem="ST")
    stv = S.now("ST")
    S.emit("sync", lambda e: None, wait=[("ST", stv)])
    S.flush()
    block_cm.__exit__(None, None, None)
    es.close()
    return nc


_CACHE = {}


def _get_nc():
    global _CACHE
    if (N_PHASES, DBG_SKIP_LN, DBG_SKIP_LN2, DBG_SKIP_LN3) != _CACHE.get('key'):
        _CACHE = {'key': (N_PHASES, DBG_SKIP_LN, DBG_SKIP_LN2, DBG_SKIP_LN3)}
    if N_PHASES not in _CACHE:
        _CACHE[N_PHASES] = build_nc(N_PHASES)
    return _CACHE[N_PHASES]


def kernel(**inputs):
    h = _host_prep(inputs)
    x = np.asarray(inputs["L0_attn_x"])
    nc = _get_nc()
    in_maps = []
    for c in range(NCORES):
        m = dict(h)
        m["x_fm"] = np.ascontiguousarray(x[0, c * TOK:(c + 1) * TOK, :].T)
        mp = np.zeros((P, NCORES), dtype=np.float32)
        mp[:, :c] = 1.0
        m["maskpre"] = mp
        in_maps.append(m)
    res = run_bass_kernel_spmd(nc, in_maps, core_ids=list(range(NCORES)))
    outs = [np.asarray(res.results[c]["out"]) for c in range(NCORES)]
    full = np.concatenate([o.T[None] for o in outs], axis=1)
    return full.reshape(1, SEQ, DM)
